# version 01
# baseline (speedup 1.0000x reference)
import contextlib
import os

import numpy as np
import concourse.bass as bass
import concourse.mybir as mybir
from concourse.bass_utils import run_bass_kernel_spmd

F32 = mybir.dt.float32
BF16 = mybir.dt.bfloat16
AF = mybir.ActivationFunctionType
ALU = mybir.AluOpType
AX = mybir.AxisListType

B, S, E = 4, 4096, 1024
NH, HD = 16, 64
TOK = 2048
I_FF, NE = 4096, 4
LN1_EPS = 1e-5
LN2_EPS = 1e-12
NEGINF = -3.0e38

QPAIRS = {0: [(j, 31 - j) for j in range(8)],
          1: [(8 + j, 23 - j) for j in range(8)]}
QLOC = {hg: [p[0] for p in QPAIRS[hg]] + [p[1] for p in QPAIRS[hg]]
        for hg in (0, 1)}

ENGS = ("sync", "tensor", "vector", "scalar", "gpsimd")


def _qrows(hg):
    out = []
    for g in QLOC[hg]:
        out.extend(range(g * 128, g * 128 + 128))
    return np.array(out)


class W:

    def __init__(self, nc, cur, eng, sems):
        self.nc = nc
        self.cur = cur
        self.e = eng
        self.sems = sems
        self.cnt = {k: 0 for k in sems}
        self.parity_reg = None

    def wait(self, name, v=None):
        v = self.cnt[name] if v is None else v
        if v > 0:
            self.e.wait_ge(self.sems[name], v)

    def op(self, eng, fn, sem=None, n=1, after=()):
        if eng == self.cur:
            for a in after:
                if isinstance(a, tuple):
                    self.wait(a[0], a[1])
                else:
                    self.wait(a)
        newv = None
        if sem is not None:
            self.cnt[sem] += n
            newv = self.cnt[sem]
        if eng == self.cur:
            inst = fn(self.e)
            if sem is not None and inst is not None:
                inst.then_inc(self.sems[sem], n)
        return newv

    def barrier(self):
        for name in self.cnt:
            self.wait(name)

    def if_hg0_begin(self):
        if self.parity_reg is None:
            self.parity_reg = self.e.to_reg(self.e.partition_id() % 2)
        self._ifg = self.e.If_eq(self.parity_reg, 0)
        self._ifg.__enter__()

    def else_begin(self):
        self._ifg.__exit__(None, None, None)
        self._elg = self.e.Else()
        self._elg.__enter__()

    def endif(self):
        self._elg.__exit__(None, None, None)


def build_program(debug=False):
    nc = bass.Bass("TRN2", target_bir_lowering=False,
                   detect_race_conditions=False)
    dram = {}

    def din(name, shape, dt=F32):
        dram[name] = nc.dram_tensor(name, list(shape), dt,
                                    kind="ExternalInput").ap()

    def dscr(name, shape, dt=BF16):
        kind = "ExternalOutput" if debug else "Internal"
        dram[name] = nc.dram_tensor(name, list(shape), dt, kind=kind).ap()

    din("x_full", (S, E))
    din("x_own", (TOK, E))
    din("wq", (E, E)); din("wk", (E, E)); din("wv", (E, E))
    din("bq", (128, 8)); din("bk", (128, 8)); din("bv", (128, 8))
    din("wprojT", (E, E)); din("bproj_row", (1, E))
    din("w_router", (E, NE))
    din("w1r", (NE, 32, 128, 1024))
    din("b1f", (128, 128))
    din("wo", (I_FF, E))
    din("bo_row", (1, E))
    din("gb", (128, E)); din("bb", (128, E))
    din("idt", (128, 128))
    din("ones_row", (1, 128))
    din("e2", (2, 128))
    din("cm4", (128, 4 * 512))
    din("epsln", (128, 2))
    dram["out"] = nc.dram_tensor("out", [TOK, E], F32,
                                 kind="ExternalOutput").ap()

    dscr("xn_scr", (S, E))
    dscr("xn_own_scr", (TOK, E))
    dscr("q_scr", (E, TOK))
    dscr("kt_scr", (E, S))
    dscr("vt_scr", (E, S))
    dscr("attn_scr", (E, TOK))
    dscr("inter_scr", (32, 4, 128, 512))
    if debug:
        for _e in range(4):
            dscr(f"int_e{_e}", (32, 4, 128, 512))
    dscr("y_scr", (TOK, E))

    stack = contextlib.ExitStack()
    sb = {}

    def sbuf(name, shape, dt=BF16, scoped=None):
        cm = nc.sbuf_tensor(name, list(shape), dt)
        if scoped is None:
            sb[name] = stack.enter_context(cm)
        else:
            sb[name] = cm.__enter__()
            scoped.append(cm)
        return sb[name]

    ps = [stack.enter_context(nc.psum_tensor(f"pb{i}", [128, 512], F32))
          for i in range(8)]

    sbuf("c_bq", (128, 8), F32); sbuf("c_bk", (128, 8), F32)
    sbuf("c_bv", (128, 8), F32)
    sbuf("c_b1f", (128, 128), F32)
    sbuf("c_gb", (128, E), F32); sbuf("c_bb", (128, E), F32)
    sbuf("c_idt", (128, 128), BF16)
    sbuf("c_ones", (1, 128), BF16)
    sbuf("c_e2", (2, 128), BF16)
    sbuf("c_cm4", (128, 4 * 512), F32)
    sbuf("c_eps", (128, 2), F32)
    sbuf("c_bproj", (1, E), BF16)
    sbuf("c_bo", (1, E), BF16)
    sbuf("c_wrt", (128, 8 * NE), BF16)
    sbuf("xnT_own", (128, 8 * TOK), BF16)
    sbuf("f_out", (128, 16 * 1024), BF16)

    a_scope = []
    sbuf("a_x", (128, 2 * E), F32, a_scope)
    sbuf("a_xn", (128, 2 * E), BF16, a_scope)
    sbuf("a_sq", (128, E), BF16, a_scope)
    sbuf("a_st", (128, 16), F32, a_scope)
    sbuf("a_xnT", (128, 8 * S), BF16, a_scope)
    sbuf("a_w", (128, 2 * 1024), BF16, a_scope)
    sbuf("a_ev", (128, 2 * 512), BF16, a_scope)
    for cm in reversed(a_scope):
        cm.__exit__(None, None, None)

    t_scope = []
    sbuf("kT", (128, 4 * S), BF16, t_scope)
    sbuf("vT", (128, 32 * 512), BF16, t_scope)
    sbuf("qt", (128, 2 * 128), BF16, t_scope)
    sbuf("srow", (128, 2 * S), BF16, t_scope)
    sbuf("prow", (128, 2 * S), BF16, t_scope)
    sbuf("mx", (128, 8), F32, t_scope)
    sbuf("negm", (128, 2), F32, t_scope)
    sbuf("lcol", (128, 2), F32, t_scope)
    sbuf("rcol", (128, 2), BF16, t_scope)
    sbuf("rrow", (2, 128), BF16, t_scope)
    sbuf("rbc", (128, 128), BF16, t_scope)
    sbuf("ptsb", (128, 4 * 256), BF16, t_scope)
    sbuf("astg", (128, 512), BF16, t_scope)
    for cm in reversed(t_scope):
        cm.__exit__(None, None, None)

    f_scope = []
    sbuf("f_w1", (128, 2 * 1024), BF16, f_scope)
    sbuf("f_h", (128, 16 * 512), BF16, f_scope)
    sbuf("f_mr0", (1, TOK), BF16, f_scope)
    sbuf("f_mr1", (1, TOK), BF16, f_scope)
    sbuf("f_mr2", (1, TOK), BF16, f_scope)
    sbuf("f_mr3", (1, TOK), BF16, f_scope)
    sbuf("f_ar0", (1, TOK), BF16, f_scope)
    sbuf("f_ar1", (1, TOK), BF16, f_scope)
    sbuf("f_ar2", (1, TOK), BF16, f_scope)
    sbuf("f_ar3", (1, TOK), BF16, f_scope)
    sbuf("f_lg", (128, 16), F32, f_scope)
    sbuf("f_msk", (128, 4), BF16, f_scope)
    for cm in reversed(f_scope):
        cm.__exit__(None, None, None)

    w_scope = []
    sbuf("w_wo", (128, 32 * 512), BF16, w_scope)
    sbuf("w_int", (128, 32 * 512), BF16, w_scope)
    sbuf("w_xn", (128, 2 * 512), BF16, w_scope)
    sbuf("w_yh", (128, 2 * 512), BF16, w_scope)
    sbuf("w_yb", (128, 2 * 1024), BF16, w_scope)
    sbuf("w_dump", (128, 1024), BF16, w_scope)
    sbuf("w_st", (128, 16), F32, w_scope)
    sbuf("w_t", (128, 1024), F32, w_scope)
    for cm in reversed(w_scope):
        cm.__exit__(None, None, None)

    c_scope = []
    sbuf("c_ac", (128, 8 * 2048), BF16, c_scope)
    sbuf("c_wp", (128, 8 * 1024), BF16, c_scope)
    sbuf("c_x", (128, 2 * E), F32, c_scope)
    sbuf("c_tmp", (128, E), F32, c_scope)
    sbuf("c_o", (128, 2 * E), F32, c_scope)
    for cm in reversed(c_scope):
        cm.__exit__(None, None, None)

    sems = {n: stack.enter_context(nc.semaphore(n))
            for n in ("ld", "sc", "gp", "pe", "dve", "act")}
    block = stack.enter_context(nc.Block())

    SB = sb

    def walk(w: W):
        for nm, dst in [("bq", "c_bq"), ("bk", "c_bk"), ("bv", "c_bv"),
                        ("b1f", "c_b1f"), ("gb", "c_gb"), ("bb", "c_bb"),
                        ("cm4", "c_cm4"), ("epsln", "c_eps")]:
            w.op("sync", lambda e, nm=nm, dst=dst: e.dma_start(
                out=SB[dst][:], in_=dram[nm][:]), sem="ld", n=16)
        for nm, dst in [("idt", "c_idt"), ("ones_row", "c_ones"),
                        ("e2", "c_e2"), ("bproj_row", "c_bproj"),
                        ("bo_row", "c_bo")]:
            w.op("gpsimd", lambda e, nm=nm, dst=dst: e.dma_start(
                out=SB[dst][:], in_=dram[nm][:]), sem="gp", n=16)
        w.op("gpsimd", lambda e: e.dma_start(
            out=SB["c_wrt"][:].rearrange("p (c n) -> p c n", c=8),
            in_=dram["w_router"].rearrange("(c p) n -> p c n", p=128)),
            sem="gp", n=16)
        consts_ld = w.cnt["ld"]
        consts_gp = w.cnt["gp"]

        def ln1_tile(i, src, scr):
            b = i % 2
            xt = SB["a_x"][:, b * E:(b + 1) * E]
            xn = SB["a_xn"][:, b * E:(b + 1) * E]
            st = SB["a_st"]
            w.op("sync", lambda e: e.dma_start(
                out=xt, in_=dram[src][i * 128:(i + 1) * 128, :]),
                sem="ld", n=16, after=["dve"])
            w.op("vector", lambda e: e.reduce_sum(st[:, 0:1], xt, axis=AX.X),
                 sem="dve", after=["ld", "dve", "act"])
            w.op("scalar", lambda e: e.activation(
                SB["a_sq"][:], xt, AF.Square, accum_out=st[:, 1:2]),
                sem="act", after=["ld", "act", "dve"])
            w.op("vector", lambda e: e.tensor_scalar_mul(
                st[:, 2:3], st[:, 0:1], -1.0 / E), sem="dve", after=["dve"])
            w.op("vector", lambda e: e.tensor_mul(
                st[:, 3:4], st[:, 2:3], st[:, 2:3]), sem="dve", after=["dve"])
            w.op("vector", lambda e: e.tensor_scalar(
                st[:, 4:5], st[:, 1:2], 1.0 / E, None, ALU.mult),
                sem="dve", after=["dve", "act"])
            w.op("vector", lambda e: e.tensor_sub(
                st[:, 5:6], st[:, 4:5], st[:, 3:4]), sem="dve", after=["dve"])
            w.op("scalar", lambda e: e.activation(
                st[:, 6:7], st[:, 5:6], AF.Ln, bias=SB["c_eps"][:, 0:1]),
                sem="act", after=["dve", "act"])
            w.op("scalar", lambda e: e.activation(
                st[:, 7:8], st[:, 6:7], AF.Exp, scale=-0.5),
                sem="act", after=["act"])
            w.op("vector", lambda e: e.tensor_scalar(
                xn, xt, st[:, 2:3], st[:, 7:8], ALU.add, ALU.mult),
                sem="dve", after=["act", "dve"])
            w.op("sync", lambda e: e.dma_start(
                out=dram[scr][i * 128:(i + 1) * 128, :], in_=xn),
                sem="ld", n=16, after=["dve"])

        for i in range(32):
            ln1_tile(i, "x_full", "xn_scr")
        for i in range(16):
            ln1_tile(i, "x_own", "xn_own_scr")

        for d in range(8):
            for tkc in range(4):
                w.op("sync", lambda e, d=d, tkc=tkc: e.dma_start_transpose(
                    SB["xnT_own"][:, d * TOK + tkc * 512:
                                  d * TOK + (tkc + 1) * 512],
                    dram["xn_own_scr"][tkc * 512:(tkc + 1) * 512,
                                       d * 128:(d + 1) * 128]),
                    sem="ld", n=16, after=[])
        for d in range(8):
            for tkc in range(8):
                w.op("sync", lambda e, d=d, tkc=tkc: e.dma_start_transpose(
                    SB["a_xnT"][:, d * S + tkc * 512:d * S + (tkc + 1) * 512],
                    dram["xn_scr"][tkc * 512:(tkc + 1) * 512,
                                   d * 128:(d + 1) * 128]),
                    sem="ld", n=16, after=[])
        xnT_ld = w.cnt["ld"]

        groups = ([("wk", oc, 8, "kt_scr", "c_bk") for oc in range(8)]
                  + [("wv", oc, 8, "vt_scr", "c_bv") for oc in range(8)]
                  + [("wq", oc, 4, "q_scr", "c_bq") for oc in range(8)])
        for gi, (wname, oc, ntkc, scr, bias) in enumerate(groups):
            buf = gi % 2
            w.op("gpsimd", lambda e, wname=wname, oc=oc, buf=buf: e.dma_start(
                out=SB["a_w"][:, buf * 1024:(buf + 1) * 1024]
                    .rearrange("p (c n) -> p c n", c=8),
                in_=dram[wname][:, oc * 128:(oc + 1) * 128]
                    .rearrange("(c p) n -> p c n", p=128)),
                sem="gp", n=16, after=["pe"])
            gpv = w.cnt["gp"]
            for tkc in range(ntkc):
                bank = ps[tkc % 2]
                ev = SB["a_ev"][:, (tkc % 2) * 512:(tkc % 2 + 1) * 512]
                for d in range(8):
                    if wname == "wq":
                        rhs = SB["xnT_own"][:, d * TOK + tkc * 512:
                                            d * TOK + tkc * 512 + 512]
                    else:
                        rhs = SB["a_xnT"][:, d * S + tkc * 512:
                                          d * S + tkc * 512 + 512]
                    w.op("tensor",
                         lambda e, d=d, rhs=rhs, bank=bank, buf=buf: e.matmul(
                             bank[:, 0:512],
                             SB["a_w"][:, buf * 1024 + d * 128:
                                       buf * 1024 + (d + 1) * 128],
                             rhs, start=(d == 0), stop=(d == 7)),
                         sem="pe" if d == 7 else None,
                         after=(["gp", ("ld", xnT_ld), "dve"]
                                if d == 0 else []))
                if wname == "wq":
                    w.op("vector",
                         lambda e, bank=bank, ev=ev, bias=bias, oc=oc:
                         e.tensor_scalar(ev, bank[:, 0:512],
                                         SB[bias][:, oc:oc + 1], 0.125,
                                         ALU.add, ALU.mult),
                         sem="dve", after=["pe", "ld", "dve"])
                else:
                    w.op("vector",
                         lambda e, bank=bank, ev=ev, bias=bias, oc=oc:
                         e.tensor_scalar(ev, bank[:, 0:512],
                                         SB[bias][:, oc:oc + 1], None,
                                         ALU.add),
                         sem="dve", after=["pe", "ld", "dve"])
                w.op("sync",
                     lambda e, ev=ev, scr=scr, oc=oc, tkc=tkc: e.dma_start(
                         out=dram[scr][oc * 128:(oc + 1) * 128,
                                       tkc * 512:(tkc + 1) * 512], in_=ev),
                     sem="ld", n=16, after=["dve"])

        w.barrier()
        if int(os.environ.get("MK_PHASES", "9")) < 2:
            return

        def attn_qtile(hp, g, tl):
            nkt = g + 1
            nch = (nkt + 3) // 4
            for p in range(4):
                qb = p % 2
                w.op("sync", lambda e, hp=hp, p=p, g=g, qb=qb: e.dma_start(
                    out=SB["qt"][:, qb * 128:(qb + 1) * 128],
                    in_=dram["q_scr"][hp * 512 + p * 128:
                                      hp * 512 + (p + 1) * 128,
                                      tl * 128:(tl + 1) * 128]),
                    sem="ld", n=16, after=["pe"])
                qld = w.cnt["ld"]
                for c in range(nch):
                    klo = c * 512
                    wdt = min(512, S - klo)
                    for ab in (0, 1):
                        bank = ps[ab]
                        lhs = SB["qt"][ab * 64:ab * 64 + 64,
                                       qb * 128:(qb + 1) * 128]
                        rhs = SB["kT"][ab * 64:ab * 64 + 64,
                                       p * S + klo:p * S + klo + wdt]
                        w.op("tensor", lambda e, bank=bank, lhs=lhs, rhs=rhs,
                             wdt=wdt: e.matmul(bank[:, 0:wdt], lhs, rhs,
                                               start=True, stop=True),
                             sem="pe",
                             after=[("ld", qld), "dve"] if c == 0 else ["dve"])
                    for ab in (0, 1):
                        bank = ps[ab]
                        dst = SB["srow"][:, ab * S + klo:ab * S + klo + wdt]
                        if c == nch - 1:
                            cms = SB["c_cm4"][:, (g % 4) * 512:
                                              (g % 4) * 512 + wdt]
                            w.op("vector", lambda e, dst=dst, bank=bank,
                                 cms=cms, wdt=wdt: e.tensor_add(
                                     dst, bank[:, 0:wdt], cms),
                                 sem="dve", after=["pe", "dve", "act"])
                        else:
                            w.op("vector", lambda e, dst=dst, bank=bank,
                                 wdt=wdt: e.tensor_copy(dst, bank[:, 0:wdt]),
                                 sem="dve", after=["pe", "dve", "act"])
                for ab in (0, 1):
                    w.op("vector", lambda e, ab=ab, nch=nch: e.reduce_max(
                        SB["mx"][:, ab:ab + 1],
                        SB["srow"][:, ab * S:ab * S + nch * 512], axis=AX.X),
                        sem="dve", after=["dve"])
                    w.op("vector", lambda e, ab=ab: e.tensor_scalar_mul(
                        SB["negm"][:, ab:ab + 1], SB["mx"][:, ab:ab + 1],
                        -1.0), sem="dve", after=["dve"])
                wdt_all = nch * 512
                for ab in (0, 1):
                    w.op("scalar", lambda e, ab=ab, wdt_all=wdt_all:
                         e.activation(SB["prow"][:, ab * S:ab * S + wdt_all],
                                      SB["srow"][:, ab * S:ab * S + wdt_all],
                                      AF.Exp, bias=SB["negm"][:, ab:ab + 1],
                                      accum_out=SB["lcol"][:, ab:ab + 1]),
                         sem="act", after=["dve", "act", "pe"])
                def _recip(e):
                    with nc.allow_low_precision(
                            reason="bf16 softmax normalizer, within tol"):
                        return e.reciprocal(SB["rcol"][:, 0:2],
                                            SB["lcol"][:, 0:2])
                w.op("vector", _recip, sem="dve", after=["act", "dve"])
                w.op("tensor", lambda e: e.transpose(
                    ps[4][:].bitcast(BF16)[0:2, 0:128], SB["rcol"][:, 0:2],
                    SB["c_idt"][:]), sem="pe", after=["dve", "act"])
                w.op("vector", lambda e: e.tensor_copy(
                    SB["rrow"][:], ps[4][:].bitcast(BF16)[0:2, 0:128]),
                    sem="dve", after=["pe", "dve"])
                for ab in (0, 1):
                    w.op("tensor", lambda e, ab=ab: e.matmul(
                        ps[4][ab * 64:ab * 64 + 64, 128:256],
                        SB["c_e2"][:, ab * 64:ab * 64 + 64],
                        SB["rrow"][:], start=True, stop=True),
                        sem="pe", after=["dve"])
                w.op("scalar", lambda e: e.activation(
                    SB["rbc"][:], ps[4][:, 128:256], AF.Copy),
                    sem="act", after=["pe", "act", "dve"])
                for kt in range(nkt):
                    sl = kt % 2
                    for ab in (0, 1):
                        w.op("tensor", lambda e, kt=kt, ab=ab, sl=sl:
                             e.transpose(
                                 ps[2][:].bitcast(BF16)[:, (sl * 2 + ab) * 256:
                                                     (sl * 2 + ab) * 256 + 128],
                                 SB["prow"][:, ab * S + kt * 128:
                                            ab * S + (kt + 1) * 128],
                                 SB["c_idt"][:]),
                             sem="pe", after=["dve", "act"])
                    w.op("vector", lambda e, sl=sl: e.tensor_copy(
                        SB["ptsb"][:, sl * 512:sl * 512 + 512],
                        ps[2][:].bitcast(BF16)[:, sl * 512:sl * 512 + 512]),
                        sem="dve", after=["pe", "dve"])
                    for ab in (0, 1):
                        w.op("tensor", lambda e, kt=kt, ab=ab, sl=sl, hp=hp,
                             p=p, nkt=nkt: e.matmul(
                                 ps[3][ab * 64:ab * 64 + 64, 0:128],
                                 SB["vT"][:, kt * 512 + (2 * p + ab) * 64:
                                          kt * 512 + (2 * p + ab) * 64 + 64],
                                 SB["ptsb"][:, sl * 512 + ab * 256:
                                            sl * 512 + ab * 256 + 128],
                                 start=(kt == 0),
                                 stop=(kt == nkt - 1),
                                 tile_position=(0, ab * 64)),
                             sem="pe", after=["dve"])
                w.op("vector", lambda e, p=p: e.scalar_tensor_tensor(
                    SB["astg"][:, p * 128:(p + 1) * 128],
                    ps[3][:, 0:128], 0.0, SB["rbc"][:], ALU.add, ALU.mult),
                    sem="dve", after=["pe", "act", "dve", "ld"])
            w.op("sync", lambda e, hp=hp, tl=tl: e.dma_start(
                out=dram["attn_scr"][hp * 512:(hp + 1) * 512,
                                     tl * 128:(tl + 1) * 128]
                    .rearrange("(b p) f -> p b f", p=128),
                in_=SB["astg"][:].rearrange("p (b f) -> p b f", b=4)),
                sem="ld", n=16, after=["dve"])

        for hp in (0, 1):
            for p in range(4):
                w.op("sync", lambda e, hp=hp, p=p: e.dma_start(
                    out=SB["kT"][:, p * S:(p + 1) * S],
                    in_=dram["kt_scr"][hp * 512 + p * 128:
                                       hp * 512 + (p + 1) * 128, :]),
                    sem="ld", n=16, after=(["pe", "dve"] if hp else []))
            for kt in range(32):
                w.op("sync", lambda e, hp=hp, kt=kt: e.dma_start_transpose(
                    SB["vT"][:, kt * 512:(kt + 1) * 512],
                    dram["vt_scr"][hp * 512:(hp + 1) * 512,
                                   kt * 128:(kt + 1) * 128]),
                    sem="ld", n=16, after=[])
            kvld = w.cnt["ld"]
            for eng in ENGS:
                w.op(eng, lambda e: None, after=[("ld", kvld)])
            for j in range(8):
                snap = dict(w.cnt)
                w.if_hg0_begin()
                ga, gb_ = QPAIRS[0][j]
                attn_qtile(hp, ga, j)
                attn_qtile(hp, gb_, 8 + j)
                cntA = dict(w.cnt)
                w.cnt = dict(snap)
                w.else_begin()
                ga, gb_ = QPAIRS[1][j]
                attn_qtile(hp, ga, j)
                attn_qtile(hp, gb_, 8 + j)
                w.endif()
                assert w.cnt == cntA, (j, hp, w.cnt, cntA)

        w.barrier()
        if int(os.environ.get("MK_PHASES", "9")) < 3:
            return

        for tt in range(16):
            for d in range(8):
                w.op("tensor", lambda e, tt=tt, d=d: e.matmul(
                    ps[7][:, 0:4],
                    SB["xnT_own"][:, d * TOK + tt * 128:
                                  d * TOK + (tt + 1) * 128],
                    SB["c_wrt"][:, d * 4:(d + 1) * 4],
                    start=(d == 0), stop=(d == 7)),
                    sem="pe" if d == 7 else None,
                    after=(["dve"] if d == 0 else []))
            w.op("vector", lambda e: e.tensor_copy(
                SB["f_lg"][:, 0:4], ps[7][:, 0:4]),
                sem="dve", after=["pe", "dve"])
            w.op("vector", lambda e: e.reduce_max(
                SB["f_lg"][:, 8:9], SB["f_lg"][:, 0:4], axis=AX.X),
                sem="dve", after=["dve"])
            w.op("vector", lambda e: e.tensor_scalar(
                SB["f_msk"][:], SB["f_lg"][:, 0:4], SB["f_lg"][:, 8:9],
                None, ALU.is_ge), sem="dve", after=["dve"])
            for ex in range(4):
                w.op("tensor", lambda e, ex=ex: e.transpose(
                    ps[7][:].bitcast(BF16)[0:1, 512 + ex * 128:
                                        512 + (ex + 1) * 128],
                    SB["f_msk"][:, ex:ex + 1], SB["c_idt"][:]),
                    sem="pe", after=["dve"])
            for ex in range(4):
                w.op("vector", lambda e, ex=ex, tt=tt: e.tensor_copy(
                    SB[f"f_mr{ex}"][0:1, tt * 128:(tt + 1) * 128],
                    ps[7][:].bitcast(BF16)[0:1, 512 + ex * 128:
                                        512 + (ex + 1) * 128]),
                    sem="dve", after=["pe", "dve"])
        for ex in range(4):
            w.op("vector", lambda e, ex=ex: e.tensor_scalar(
                SB[f"f_ar{ex}"][0:1, :], SB[f"f_mr{ex}"][0:1, :],
                -1.0, 1.0e4, ALU.add, ALU.mult),
                sem="dve", after=["dve"])

        for icb in range(32):
            for ex in range(4):
                buf = ex % 2
                w.op("gpsimd", lambda e, ex=ex, icb=icb, buf=buf: e.dma_start(
                    out=SB["f_w1"][:, buf * 1024:(buf + 1) * 1024],
                    in_=dram["w1r"][ex, icb]), sem="gp", n=16, after=["pe"])
                gpv = w.cnt["gp"]
                for tc in range(4):
                    bank = ps[5 + (tc % 2)]
                    w.op("tensor", lambda e, ex=ex, tc=tc, bank=bank: e.matmul(
                        bank[:, 0:512], SB["c_ones"][0:1, :],
                        SB[f"f_ar{ex}"][0:1, tc * 512:(tc + 1) * 512],
                        start=True, stop=False),
                        after=["act", ("gp", gpv), "dve"])
                    for d in range(8):
                        w.op("tensor", lambda e, d=d, tc=tc, buf=buf,
                             bank=bank: e.matmul(
                                 bank[:, 0:512],
                                 SB["f_w1"][:, buf * 1024 + d * 128:
                                            buf * 1024 + (d + 1) * 128],
                                 SB["xnT_own"][:, d * TOK + tc * 512:
                                               d * TOK + (tc + 1) * 512],
                                 start=False, stop=(d == 7)),
                             sem="pe" if d == 7 else None)
                    w.op("scalar", lambda e, ex=ex, tc=tc, icb=icb, bank=bank:
                         e.activation(
                             SB["f_h"][:, (ex * 4 + tc) * 512:
                                       (ex * 4 + tc + 1) * 512],
                             bank[:, 0:512], AF.Gelu,
                             bias=SB["c_b1f"][:, ex * 32 + icb:
                                              ex * 32 + icb + 1]),
                         sem="act", after=["pe", "act", "dve", "gp"])
            for tc in range(4):
                if debug:
                    for ex in range(4):
                        w.op("gpsimd", lambda e, icb=icb, tc=tc, ex=ex:
                             e.dma_start(
                                 out=dram[f"int_e{ex}"][icb, tc],
                                 in_=SB["f_h"][:, (ex * 4 + tc) * 512:
                                               (ex * 4 + tc + 1) * 512]),
                             sem="gp", n=16, after=["act"])
                for ex in (1, 2, 3):
                    w.op("vector", lambda e, tc=tc, ex=ex: e.tensor_add(
                        SB["f_h"][:, (0 * 4 + tc) * 512:(0 * 4 + tc + 1) * 512],
                        SB["f_h"][:, (0 * 4 + tc) * 512:(0 * 4 + tc + 1) * 512],
                        SB["f_h"][:, (ex * 4 + tc) * 512:
                                  (ex * 4 + tc + 1) * 512]),
                        sem="dve", after=["act", "dve", "gp"])
                w.op("gpsimd", lambda e, icb=icb, tc=tc: e.dma_start(
                    out=dram["inter_scr"][icb, tc],
                    in_=SB["f_h"][:, tc * 512:(tc + 1) * 512]),
                    sem="gp", n=16, after=["dve"])

        w.barrier()
        if int(os.environ.get("MK_PHASES", "9")) < 4:
            return

        for hh in (0, 1):
            for icb in range(32):
                w.op("gpsimd", lambda e, icb=icb, hh=hh: e.dma_start(
                    out=SB["w_wo"][:, icb * 512:(icb + 1) * 512],
                    in_=dram["wo"][icb * 128:(icb + 1) * 128,
                                   hh * 512:(hh + 1) * 512]),
                    sem="gp", n=16, after=["pe"])
            woldv = w.cnt["gp"]
            for tc in range(4):
                for icb in range(32):
                    w.op("sync", lambda e, icb=icb, tc=tc: e.dma_start(
                        out=SB["w_int"][:, icb * 512:(icb + 1) * 512],
                        in_=dram["inter_scr"][icb, tc]),
                        sem="ld", n=16, after=["pe", "gp"])
                intld = w.cnt["ld"]
                for ttl in range(4):
                    tt = tc * 4 + ttl
                    bank = ps[5 + ttl % 2]
                    w.op("sync", lambda e, tt=tt, hh=hh, ttl=ttl:
                         e.dma_start(
                             out=SB["w_xn"][:, (ttl % 2) * 512:
                                            (ttl % 2 + 1) * 512],
                             in_=dram["xn_own_scr"][tt * 128:(tt + 1) * 128,
                                                    hh * 512:(hh + 1) * 512]),
                         sem="ld", n=16, after=["dve"])
                    xnldv = w.cnt["ld"]
                    w.op("tensor", lambda e, hh=hh, bank=bank: e.matmul(
                        bank[:, 0:512], SB["c_ones"][0:1, :],
                        SB["c_bo"][0:1, hh * 512:(hh + 1) * 512],
                        start=True, stop=False),
                        after=[("gp", woldv), ("ld", intld), "dve"])
                    for icb in range(32):
                        w.op("tensor", lambda e, icb=icb, ttl=ttl, bank=bank:
                             e.matmul(
                                 bank[:, 0:512],
                                 SB["w_int"][:, icb * 512 + ttl * 128:
                                             icb * 512 + ttl * 128 + 128],
                                 SB["w_wo"][:, icb * 512:(icb + 1) * 512],
                                 start=False, stop=(icb == 31)),
                             sem="pe" if icb == 31 else None)
                    w.op("vector", lambda e, ttl=ttl, bank=bank: e.tensor_add(
                        SB["w_yh"][:, (ttl % 2) * 512:(ttl % 2 + 1) * 512],
                        bank[:, 0:512],
                        SB["w_xn"][:, (ttl % 2) * 512:(ttl % 2 + 1) * 512]),
                        sem="dve", after=["pe", ("ld", xnldv), "dve"])
                    w.op("sync", lambda e, tt=tt, hh=hh, ttl=ttl: e.dma_start(
                        out=dram["y_scr"][tt * 128:(tt + 1) * 128,
                                          hh * 512:(hh + 1) * 512],
                        in_=SB["w_yh"][:, (ttl % 2) * 512:
                                       (ttl % 2 + 1) * 512]),
                        sem="ld", n=16, after=["dve"])

        for tt in range(16):
            b = tt % 2
            yb = SB["w_yb"][:, b * 1024:(b + 1) * 1024]
            st = SB["w_st"]
            w.op("sync", lambda e, tt=tt, yb=yb: e.dma_start(
                out=yb, in_=dram["y_scr"][tt * 128:(tt + 1) * 128, :]),
                sem="ld", n=16, after=["dve", "act"])
            w.op("vector", lambda e, yb=yb: e.reduce_sum(
                st[:, 0:1], yb, axis=AX.X), sem="dve",
                after=["ld", "dve", "act"])
            w.op("scalar", lambda e, yb=yb: e.activation(
                SB["w_dump"][:], yb, AF.Square, accum_out=st[:, 1:2]),
                sem="act", after=["ld", "act", "dve"])
            w.op("vector", lambda e: e.tensor_scalar_mul(
                st[:, 2:3], st[:, 0:1], -1.0 / E), sem="dve", after=["dve"])
            w.op("vector", lambda e: e.tensor_mul(
                st[:, 3:4], st[:, 2:3], st[:, 2:3]), sem="dve", after=["dve"])
            w.op("vector", lambda e: e.tensor_scalar(
                st[:, 4:5], st[:, 1:2], 1.0 / E, None, ALU.mult),
                sem="dve", after=["dve", "act"])
            w.op("vector", lambda e: e.tensor_sub(
                st[:, 5:6], st[:, 4:5], st[:, 3:4]), sem="dve", after=["dve"])
            w.op("scalar", lambda e: e.activation(
                st[:, 6:7], st[:, 5:6], AF.Ln, bias=SB["c_eps"][:, 1:2]),
                sem="act", after=["dve", "act"])
            w.op("scalar", lambda e: e.activation(
                st[:, 7:8], st[:, 6:7], AF.Exp, scale=-0.5),
                sem="act", after=["act"])
            w.op("vector", lambda e, yb=yb: e.tensor_scalar(
                SB["w_t"][:], yb, st[:, 2:3], st[:, 7:8], ALU.add, ALU.mult),
                sem="dve", after=["act", "dve"])
            w.op("vector", lambda e: e.tensor_mul(
                SB["w_t"][:], SB["w_t"][:], SB["c_gb"][:]),
                sem="dve", after=["dve"])
            w.op("vector", lambda e, tt=tt: e.tensor_add(
                SB["f_out"][:, tt * 1024:(tt + 1) * 1024],
                SB["w_t"][:], SB["c_bb"][:]), sem="dve", after=["dve"])

        w.barrier()
        if int(os.environ.get("MK_PHASES", "9")) < 5:
            return

        for ac in range(8):
            w.op("sync", lambda e, ac=ac: e.dma_start(
                out=SB["c_ac"][:, ac * 2048:(ac + 1) * 2048],
                in_=dram["attn_scr"][ac * 128:(ac + 1) * 128, :]),
                sem="ld", n=16, after=[])
        w.op("gpsimd", lambda e: e.dma_start(
            out=SB["c_wp"][:].rearrange("p (c n) -> p c n", c=8),
            in_=dram["wprojT"].rearrange("(c p) n -> p c n", p=128)),
            sem="gp", n=16, after=[])
        cldv = w.cnt["ld"]
        cgpv = w.cnt["gp"]
        for tt in range(16):
            b = tt % 2
            xo = SB["c_x"][:, b * E:(b + 1) * E]
            oo = SB["c_o"][:, b * E:(b + 1) * E]
            w.op("sync", lambda e, tt=tt, xo=xo: e.dma_start(
                out=xo, in_=dram["x_own"][tt * 128:(tt + 1) * 128, :]),
                sem="ld", n=16, after=["sc"])
            xldv = w.cnt["ld"]
            for hh in (0, 1):
                bank = ps[hh]
                w.op("tensor", lambda e, hh=hh, bank=bank: e.matmul(
                    bank[:, 0:512], SB["c_ones"][0:1, :],
                    SB["c_bproj"][0:1, hh * 512:(hh + 1) * 512],
                    start=True, stop=False),
                    after=[("ld", cldv), ("gp", cgpv), "dve"])
                for ac in range(8):
                    w.op("tensor", lambda e, ac=ac, hh=hh, tt=tt, bank=bank:
                         e.matmul(
                             bank[:, 0:512],
                             SB["c_ac"][:, ac * 2048 + tt * 128:
                                        ac * 2048 + (tt + 1) * 128],
                             SB["c_wp"][:, ac * 1024 + hh * 512:
                                        ac * 1024 + (hh + 1) * 512],
                             start=False, stop=(ac == 7)),
                         sem="pe" if ac == 7 else None)
                w.op("vector", lambda e, hh=hh, xo=xo, bank=bank:
                     e.tensor_add(SB["c_tmp"][:, hh * 512:(hh + 1) * 512],
                                  bank[:, 0:512],
                                  xo[:, hh * 512:(hh + 1) * 512]),
                     sem="dve", after=["pe", ("ld", xldv), "dve"])
                w.op("vector", lambda e, hh=hh, tt=tt, oo=oo: e.tensor_add(
                    oo[:, hh * 512:(hh + 1) * 512],
                    SB["c_tmp"][:, hh * 512:(hh + 1) * 512],
                    SB["f_out"][:, tt * 1024 + hh * 512:
                                tt * 1024 + (hh + 1) * 512]),
                    sem="dve", after=["dve"])
            w.op("scalar", lambda e, tt=tt, oo=oo: e.dma_start(
                out=dram["out"][tt * 128:(tt + 1) * 128, :], in_=oo),
                sem="sc", n=16, after=["dve"])

        w.barrier()

    with stack:
        @block.sync
        def _(eng):
            walk(W(nc, "sync", eng, sems))

        @block.tensor
        def _(eng):
            walk(W(nc, "tensor", eng, sems))

        @block.vector
        def _(eng):
            walk(W(nc, "vector", eng, sems))

        @block.scalar
        def _(eng):
            walk(W(nc, "scalar", eng, sems))

        @block.gpsimd
        def _(eng):
            walk(W(nc, "gpsimd", eng, sems))

    nc.finalize()
    return nc


_NC_CACHE = {}


def _get_nc(debug=False):
    if debug not in _NC_CACHE:
        _NC_CACHE[debug] = build_program(debug)
    return _NC_CACHE[debug]


def _cm4():
    f32 = np.float32
    cm = np.zeros((128, 4 * 512), dtype=f32)
    p = np.arange(128).reshape(128, 1)
    f = np.arange(512).reshape(1, 512)
    for j in range(4):
        cm[:, j * 512:(j + 1) * 512] = np.where(f <= p + 128 * j, 0.0, -1e9)
    return cm


def host_inputs(inputs, debug=False):
    x = np.asarray(inputs["L0_attn_hidden_states"])
    w_attn = np.asarray(inputs["L0_attn_w_attn"])
    b_attn = np.asarray(inputs["L0_attn_b_attn"])
    w_proj = np.asarray(inputs["L0_attn_w_proj"])
    b_proj = np.asarray(inputs["L0_attn_b_proj"])
    W_router = np.asarray(inputs["L0_ffn_W_router"])
    W1 = np.asarray(inputs["L0_ffn_W1"])
    b1 = np.asarray(inputs["L0_ffn_b1"])
    Wo = np.asarray(inputs["L0_ffn_Wo"])
    bo = np.asarray(inputs["L0_ffn_bo"])
    gam = np.asarray(inputs["L0_ffn_ln_gamma"])
    bet = np.asarray(inputs["L0_ffn_ln_beta"])

    f32 = np.float32
    com = {
        "wq": np.ascontiguousarray(w_attn[0:E].T, dtype=f32),
        "wk": np.ascontiguousarray(w_attn[E:2 * E].T, dtype=f32),
        "wv": np.ascontiguousarray(w_attn[2 * E:3 * E].T, dtype=f32),
        "bq": np.ascontiguousarray(b_attn[0:E].reshape(8, 128).T, dtype=f32),
        "bk": np.ascontiguousarray(b_attn[E:2 * E].reshape(8, 128).T,
                                   dtype=f32),
        "bv": np.ascontiguousarray(b_attn[2 * E:3 * E].reshape(8, 128).T,
                                   dtype=f32),
        "wprojT": np.ascontiguousarray(w_proj.T, dtype=f32),
        "bproj_row": np.ascontiguousarray(b_proj.reshape(1, E), dtype=f32),
        "w_router": np.ascontiguousarray(W_router, dtype=f32),
        "w1r": np.ascontiguousarray(
            W1.reshape(NE, 8, 128, 32, 128).transpose(0, 3, 2, 1, 4)
              .reshape(NE, 32, 128, 1024), dtype=f32),
        "b1f": np.ascontiguousarray(
            b1.reshape(NE, 32, 128).transpose(2, 0, 1).reshape(128, 128),
            dtype=f32),
        "wo": np.ascontiguousarray(Wo, dtype=f32),
        "bo_row": np.ascontiguousarray(bo.reshape(1, E), dtype=f32),
        "gb": np.ascontiguousarray(np.tile(gam.reshape(1, E), (128, 1)),
                                   dtype=f32),
        "bb": np.ascontiguousarray(np.tile(bet.reshape(1, E), (128, 1)),
                                   dtype=f32),
        "idt": np.eye(128, dtype=f32),
        "ones_row": np.ones((1, 128), dtype=f32),
        "e2": np.concatenate(
            [np.tile(np.array([[1.0], [0.0]], f32), (1, 64)),
             np.tile(np.array([[0.0], [1.0]], f32), (1, 64))],
            axis=1),
        "cm4": _cm4(),
        "epsln": np.tile(np.array([[LN1_EPS, LN2_EPS]], f32), (128, 1)),
    }
    maps = []
    for c in range(8):
        b_, hg = c // 2, c % 2
        rows = _qrows(hg)
        m = dict(com)
        m["x_full"] = np.ascontiguousarray(x[b_], dtype=f32)
        m["x_own"] = np.ascontiguousarray(x[b_][rows], dtype=f32)
        maps.append(m)
    return maps


def kernel(**inputs):
    nc = _get_nc(False)
    in_maps = host_inputs(inputs)
    res = run_bass_kernel_spmd(nc, in_maps, core_ids=list(range(8)))
    out = np.empty((B, S, E), dtype=np.float32)
    for c in range(8):
        b_, hg = c // 2, c % 2
        out[b_][_qrows(hg)] = res.results[c]["out"]
    return out


# version 02
# speedup vs baseline: 1.7014x; 1.7014x over previous
import contextlib
import copy as _copy
import os

import numpy as np
import concourse.bass as bass
import concourse.mybir as mybir
from concourse.bass_utils import run_bass_kernel_spmd

F32 = mybir.dt.float32
BF16 = mybir.dt.bfloat16
AF = mybir.ActivationFunctionType
ALU = mybir.AluOpType
AX = mybir.AxisListType

B, S, E = 4, 4096, 1024
NH, HD = 16, 64
TOK = 2048
I_FF, NE = 4096, 4
LN1_EPS = 1e-5
LN2_EPS = 1e-12

QPAIRS = {0: [(j, 31 - j) for j in range(8)],
          1: [(8 + j, 23 - j) for j in range(8)]}
QLOC = {hg: [p[0] for p in QPAIRS[hg]] + [p[1] for p in QPAIRS[hg]]
        for hg in (0, 1)}


def _qrows(hg):
    out = []
    for g in QLOC[hg]:
        out.extend(range(g * 128, g * 128 + 128))
    return np.array(out)


class W:

    def __init__(self, nc, cur, eng, sems):
        self.nc = nc
        self.cur = cur
        self.e = eng
        self.sems = sems
        self.cnt = {k: 0 for k in sems}
        self.res = {}
        self.pending = {}
        self.wm = {k: 0 for k in sems}
        self.parity_reg = None

    def op(self, eng, fn, sem=None, n=1, reads=(), writes=()):
        waits = {}

        def merge(d):
            for s, v in d.items():
                if v > waits.get(s, 0):
                    waits[s] = v

        for t in reads:
            r = self.res.get(t)
            if r:
                merge(r[0])
        for t in writes:
            r = self.res.get(t)
            if r:
                merge(r[0])
                merge(r[1])
        if eng == self.cur:
            for s, v in sorted(waits.items()):
                if v > self.wm[s]:
                    self.e.wait_ge(self.sems[s], v)
                    self.wm[s] = v
        newv = None
        if sem is not None:
            self.cnt[sem] += n
            newv = self.cnt[sem]
        if eng == self.cur:
            inst = fn(self.e)
            if sem is not None and inst is not None:
                inst.then_inc(self.sems[sem], n)
        if sem is not None:
            allw = list(writes)
            allr = list(reads)
            for pr, pw in self.pending.pop(eng, []):
                allr.extend(pr)
                allw.extend(pw)
            for t in allw:
                self.res[t] = ({sem: newv}, {})
            for t in allr:
                if t in self.res:
                    rd = self.res[t][1]
                    if newv > rd.get(sem, 0):
                        rd[sem] = newv
        else:
            self.pending.setdefault(eng, []).append(
                (list(reads), list(writes)))
        return newv

    def barrier(self):
        for s in self.cnt:
            if self.cnt[s] > self.wm[s]:
                self.e.wait_ge(self.sems[s], self.cnt[s])
                self.wm[s] = self.cnt[s]
        self.res = {}
        self.pending = {}

    def snap(self):
        return (dict(self.cnt), _copy.deepcopy(self.res),
                _copy.deepcopy(self.pending), dict(self.wm))

    def restore(self, s):
        self.cnt = dict(s[0])
        self.res = _copy.deepcopy(s[1])
        self.pending = _copy.deepcopy(s[2])
        self.wm = dict(s[3])

    def merge_arm(self, other_cnt, other_res, other_wm):
        assert self.cnt == other_cnt, (self.cnt, other_cnt)
        tags = set(self.res) | set(other_res)
        merged = {}
        for t in tags:
            w1, r1 = self.res.get(t, ({}, {}))
            w2, r2 = other_res.get(t, ({}, {}))
            wv = dict(w1)
            for s, v in w2.items():
                wv[s] = max(wv.get(s, 0), v)
            rv = dict(r1)
            for s, v in r2.items():
                rv[s] = max(rv.get(s, 0), v)
            merged[t] = (wv, rv)
        self.res = merged
        self.wm = {s: min(self.wm[s], other_wm[s]) for s in self.wm}

    def if_hg0_begin(self):
        if self.parity_reg is None:
            self.parity_reg = self.e.to_reg(self.e.partition_id() % 2)
        self._ifg = self.e.If_eq(self.parity_reg, 0)
        self._ifg.__enter__()

    def else_begin(self):
        self._ifg.__exit__(None, None, None)
        self._elg = self.e.Else()
        self._elg.__enter__()

    def endif(self):
        self._elg.__exit__(None, None, None)


def build_program(debug=False):
    nc = bass.Bass("TRN2", target_bir_lowering=False,
                   detect_race_conditions=False)
    dram = {}

    def din(name, shape, dt=F32):
        dram[name] = nc.dram_tensor(name, list(shape), dt,
                                    kind="ExternalInput").ap()

    def dscr(name, shape, dt=BF16):
        kind = "ExternalOutput" if debug else "Internal"
        dram[name] = nc.dram_tensor(name, list(shape), dt, kind=kind).ap()

    din("x_full", (S, E))
    din("x_own", (TOK, E))
    din("wq", (E, E)); din("wk", (E, E)); din("wv", (E, E))
    din("bq", (128, 8)); din("bk", (128, 8)); din("bv", (128, 8))
    din("wprojT", (E, E)); din("bproj_row", (1, E))
    din("w_router", (E, NE))
    din("w1r", (NE, 32, 128, 1024))
    din("b1f", (128, 128))
    din("wo", (I_FF, E))
    din("bo_row", (1, E))
    din("gb", (128, E)); din("bb", (128, E))
    din("idt", (128, 128))
    din("ones_row", (1, 128))
    din("e2", (2, 128))
    din("cm4", (128, 4 * 512))
    din("epsln", (128, 2))
    dram["out"] = nc.dram_tensor("out", [TOK, E], F32,
                                 kind="ExternalOutput").ap()

    dscr("xn_scr", (S, E))
    dscr("xn_own_scr", (TOK, E))
    dscr("q_scr", (E, TOK))
    dscr("kt_scr", (E, S))
    dscr("vt_scr", (E, S))
    dscr("attn_scr", (E, TOK))
    dscr("inter_scr", (32, 4, 128, 512))
    dscr("y_scr", (TOK, E))
    if debug:
        for _e in range(4):
            dscr(f"int_e{_e}", (32, 4, 128, 512))

    stack = contextlib.ExitStack()
    sb = {}

    def sbuf(name, shape, dt=BF16, scoped=None):
        cm = nc.sbuf_tensor(name, list(shape), dt)
        if scoped is None:
            sb[name] = stack.enter_context(cm)
        else:
            sb[name] = cm.__enter__()
            scoped.append(cm)
        return sb[name]

    ps = [stack.enter_context(nc.psum_tensor(f"pb{i}", [128, 512], F32))
          for i in range(8)]

    sbuf("c_bq", (128, 8), F32); sbuf("c_bk", (128, 8), F32)
    sbuf("c_bv", (128, 8), F32)
    sbuf("c_b1f", (128, 128), F32)
    sbuf("c_gb", (128, E), F32); sbuf("c_bb", (128, E), F32)
    sbuf("c_idt", (128, 128), BF16)
    sbuf("c_ones", (1, 128), BF16)
    sbuf("c_e2", (2, 128), BF16)
    sbuf("c_cm4", (128, 4 * 512), F32)
    sbuf("c_eps", (128, 2), F32)
    sbuf("c_bproj", (1, E), BF16)
    sbuf("c_bo", (1, E), BF16)
    sbuf("c_wrt", (128, 8 * NE), BF16)
    sbuf("xnT_own", (128, 8 * TOK), BF16)
    sbuf("f_out", (128, 16 * 1024), BF16)

    a_scope = []
    sbuf("a_x", (128, 3 * E), F32, a_scope)
    sbuf("a_xn", (128, 3 * E), BF16, a_scope)
    sbuf("a_sq", (128, E), BF16, a_scope)
    sbuf("a_st", (128, 3 * 16), F32, a_scope)
    sbuf("a_xnT", (128, 8 * S), BF16, a_scope)
    sbuf("a_w", (128, 3 * 1024), BF16, a_scope)
    sbuf("a_ev", (128, 4 * 512), BF16, a_scope)
    for cm in reversed(a_scope):
        cm.__exit__(None, None, None)

    t_scope = []
    sbuf("kT", (128, 4 * S), BF16, t_scope)
    sbuf("vT", (128, 32 * 512), BF16, t_scope)
    sbuf("qt", (128, 4 * 128), BF16, t_scope)
    sbuf("srow", (128, 2 * 4608), BF16, t_scope)
    sbuf("prow", (128, 2 * 4608), BF16, t_scope)
    sbuf("mx", (128, 8), F32, t_scope)
    sbuf("negm", (128, 4), F32, t_scope)
    sbuf("lcol", (128, 4), F32, t_scope)
    sbuf("rcol", (128, 4), BF16, t_scope)
    sbuf("rrow", (2, 2 * 128), BF16, t_scope)
    sbuf("rbc", (128, 2 * 128), BF16, t_scope)
    sbuf("ptsb", (128, 4 * 256), BF16, t_scope)
    sbuf("astg", (128, 2 * 512), BF16, t_scope)
    for cm in reversed(t_scope):
        cm.__exit__(None, None, None)

    f_scope = []
    sbuf("f_w1", (128, 3 * 1024), BF16, f_scope)
    sbuf("f_h", (128, 16 * 512), BF16, f_scope)
    sbuf("f_mr0", (1, TOK), BF16, f_scope)
    sbuf("f_mr1", (1, TOK), BF16, f_scope)
    sbuf("f_mr2", (1, TOK), BF16, f_scope)
    sbuf("f_mr3", (1, TOK), BF16, f_scope)
    sbuf("f_ar0", (1, TOK), BF16, f_scope)
    sbuf("f_ar1", (1, TOK), BF16, f_scope)
    sbuf("f_ar2", (1, TOK), BF16, f_scope)
    sbuf("f_ar3", (1, TOK), BF16, f_scope)
    sbuf("f_lg", (128, 16), F32, f_scope)
    sbuf("f_msk", (128, 4), BF16, f_scope)
    for cm in reversed(f_scope):
        cm.__exit__(None, None, None)

    w_scope = []
    sbuf("w_wo", (128, 32 * 512), BF16, w_scope)
    sbuf("w_int", (128, 32 * 512), BF16, w_scope)
    sbuf("w_xn", (128, 4 * 512), BF16, w_scope)
    sbuf("w_yh", (128, 4 * 512), BF16, w_scope)
    sbuf("w_yb", (128, 2 * 1024), BF16, w_scope)
    sbuf("w_dump", (128, 1024), BF16, w_scope)
    sbuf("w_st", (128, 2 * 16), F32, w_scope)
    sbuf("w_t", (128, 2 * 1024), F32, w_scope)
    for cm in reversed(w_scope):
        cm.__exit__(None, None, None)

    c_scope = []
    sbuf("c_ac", (128, 8 * 2048), BF16, c_scope)
    sbuf("c_wp", (128, 8 * 1024), BF16, c_scope)
    sbuf("c_x", (128, 2 * E), F32, c_scope)
    sbuf("c_tmp", (128, 2 * E), F32, c_scope)
    sbuf("c_o", (128, 2 * E), F32, c_scope)
    for cm in reversed(c_scope):
        cm.__exit__(None, None, None)

    sems = {n: stack.enter_context(nc.semaphore(n))
            for n in ("ld", "sc", "gp", "pe", "dve", "act")}
    block = stack.enter_context(nc.Block())

    SB = sb
    PHASES = int(os.environ.get("MK_PHASES", "9"))

    def walk(w: W):
        for nm, dst in [("bq", "c_bq"), ("bk", "c_bk"), ("bv", "c_bv"),
                        ("b1f", "c_b1f"), ("gb", "c_gb"), ("bb", "c_bb"),
                        ("cm4", "c_cm4"), ("epsln", "c_eps")]:
            w.op("sync", lambda e, nm=nm, dst=dst: e.dma_start(
                out=SB[dst][:], in_=dram[nm][:]), sem="ld", n=16,
                writes=[dst])
        for nm, dst in [("idt", "c_idt"), ("ones_row", "c_ones"),
                        ("e2", "c_e2"), ("bproj_row", "c_bproj"),
                        ("bo_row", "c_bo")]:
            w.op("gpsimd", lambda e, nm=nm, dst=dst: e.dma_start(
                out=SB[dst][:], in_=dram[nm][:]), sem="gp", n=16,
                writes=[dst])
        w.op("gpsimd", lambda e: e.dma_start(
            out=SB["c_wrt"][:].rearrange("p (c n) -> p c n", c=8),
            in_=dram["w_router"].rearrange("(c p) n -> p c n", p=128)),
            sem="gp", n=16, writes=["c_wrt"])

        def ln1_tile(i, src, scr):
            b = i % 3
            xt = SB["a_x"][:, b * E:(b + 1) * E]
            xn = SB["a_xn"][:, b * E:(b + 1) * E]
            st = SB["a_st"][:, b * 16:(b + 1) * 16]
            tx, txn = ("a_x", b), ("a_xn", b)
            tst = lambda k: ("a_st", b, k)
            w.op("sync", lambda e: e.dma_start(
                out=xt, in_=dram[src][i * 128:(i + 1) * 128, :]),
                sem="ld", n=16, writes=[tx])
            w.op("vector", lambda e: e.reduce_sum(st[:, 0:1], xt, axis=AX.X),
                 sem="dve", reads=[tx], writes=[tst(0)])
            w.op("scalar", lambda e: e.activation(
                SB["a_sq"][:], xt, AF.Square, accum_out=st[:, 1:2]),
                sem="act", reads=[tx], writes=["a_sq", tst(1)])
            w.op("vector", lambda e: e.tensor_scalar_mul(
                st[:, 2:3], st[:, 0:1], -1.0 / E), sem="dve",
                reads=[tst(0)], writes=[tst(2)])
            w.op("vector", lambda e: e.tensor_mul(
                st[:, 3:4], st[:, 2:3], st[:, 2:3]), sem="dve",
                reads=[tst(2)], writes=[tst(3)])
            w.op("vector", lambda e: e.tensor_scalar(
                st[:, 4:5], st[:, 1:2], 1.0 / E, None, ALU.mult),
                sem="dve", reads=[tst(1)], writes=[tst(4)])
            w.op("vector", lambda e: e.tensor_sub(
                st[:, 5:6], st[:, 4:5], st[:, 3:4]), sem="dve",
                reads=[tst(3), tst(4)], writes=[tst(5)])
            w.op("scalar", lambda e: e.activation(
                st[:, 6:7], st[:, 5:6], AF.Ln, bias=SB["c_eps"][:, 0:1]),
                sem="act", reads=[tst(5), "c_eps"], writes=[tst(6)])
            w.op("scalar", lambda e: e.activation(
                st[:, 7:8], st[:, 6:7], AF.Exp, scale=-0.5),
                sem="act", reads=[tst(6)], writes=[tst(7)])
            w.op("vector", lambda e: e.tensor_scalar(
                xn, xt, st[:, 2:3], st[:, 7:8], ALU.add, ALU.mult),
                sem="dve", reads=[tx, tst(2), tst(7)], writes=[txn])
            w.op("sync", lambda e: e.dma_start(
                out=dram[scr][i * 128:(i + 1) * 128, :], in_=xn),
                sem="ld", n=16, reads=[txn], writes=[("scr", scr, i)])

        for i in range(32):
            ln1_tile(i, "x_full", "xn_scr")
        for i in range(16):
            ln1_tile(i, "x_own", "xn_own_scr")

        for d in range(8):
            for tkc in range(4):
                w.op("sync", lambda e, d=d, tkc=tkc: e.dma_start_transpose(
                    SB["xnT_own"][:, d * TOK + tkc * 512:
                                  d * TOK + (tkc + 1) * 512],
                    dram["xn_own_scr"][tkc * 512:(tkc + 1) * 512,
                                       d * 128:(d + 1) * 128]),
                    sem="ld", n=16,
                    reads=[("scr", "xn_own_scr", tkc * 4 + k)
                           for k in range(4)],
                    writes=[("xnT_own", d, tkc)])
        for d in range(8):
            for tkc in range(8):
                w.op("sync", lambda e, d=d, tkc=tkc: e.dma_start_transpose(
                    SB["a_xnT"][:, d * S + tkc * 512:d * S + (tkc + 1) * 512],
                    dram["xn_scr"][tkc * 512:(tkc + 1) * 512,
                                   d * 128:(d + 1) * 128]),
                    sem="ld", n=16,
                    reads=[("scr", "xn_scr", tkc * 4 + k) for k in range(4)],
                    writes=[("a_xnT", d, tkc)])

        groups = ([("wk", oc, 8, "kt_scr", "c_bk") for oc in range(8)]
                  + [("wv", oc, 8, "vt_scr", "c_bv") for oc in range(8)]
                  + [("wq", oc, 4, "q_scr", "c_bq") for oc in range(8)])
        for gi, (wname, oc, ntkc, scr, bias) in enumerate(groups):
            buf = gi % 3
            w.op("gpsimd", lambda e, wname=wname, oc=oc, buf=buf: e.dma_start(
                out=SB["a_w"][:, buf * 1024:(buf + 1) * 1024]
                    .rearrange("p (c n) -> p c n", c=8),
                in_=dram[wname][:, oc * 128:(oc + 1) * 128]
                    .rearrange("(c p) n -> p c n", p=128)),
                sem="gp", n=16, writes=[("a_w", buf)])
            for tkc in range(ntkc):
                bk = (gi * 8 + tkc) % 4
                bank = ps[bk]
                ev = SB["a_ev"][:, bk * 512:(bk + 1) * 512]
                rds = [("a_w", buf)]
                if wname == "wq":
                    rds += [("xnT_own", d, tkc) for d in range(8)]
                else:
                    rds += [("a_xnT", d, tkc) for d in range(8)]
                for d in range(8):
                    if wname == "wq":
                        rhs = SB["xnT_own"][:, d * TOK + tkc * 512:
                                            d * TOK + tkc * 512 + 512]
                    else:
                        rhs = SB["a_xnT"][:, d * S + tkc * 512:
                                          d * S + tkc * 512 + 512]
                    w.op("tensor",
                         lambda e, d=d, rhs=rhs, bank=bank, buf=buf: e.matmul(
                             bank[:, 0:512],
                             SB["a_w"][:, buf * 1024 + d * 128:
                                       buf * 1024 + (d + 1) * 128],
                             rhs, start=(d == 0), stop=(d == 7)),
                         sem="pe" if d == 7 else None,
                         reads=rds if d == 0 else (),
                         writes=([("ps", bk)] if d in (0, 7) else ()))
                sc2 = 0.125 if wname == "wq" else None
                op1 = ALU.mult if wname == "wq" else ALU.bypass
                if wname == "wq":
                    w.op("vector",
                         lambda e, bank=bank, ev=ev, bias=bias, oc=oc:
                         e.tensor_scalar(ev, bank[:, 0:512],
                                         SB[bias][:, oc:oc + 1], 0.125,
                                         ALU.add, ALU.mult),
                         sem="dve", reads=[("ps", bk), bias],
                         writes=[("a_ev", bk)])
                else:
                    w.op("vector",
                         lambda e, bank=bank, ev=ev, bias=bias, oc=oc:
                         e.tensor_scalar(ev, bank[:, 0:512],
                                         SB[bias][:, oc:oc + 1], None,
                                         ALU.add),
                         sem="dve", reads=[("ps", bk), bias],
                         writes=[("a_ev", bk)])
                w.op("sync",
                     lambda e, ev=ev, scr=scr, oc=oc, tkc=tkc: e.dma_start(
                         out=dram[scr][oc * 128:(oc + 1) * 128,
                                       tkc * 512:(tkc + 1) * 512], in_=ev),
                     sem="ld", n=16, reads=[("a_ev", bk)],
                     writes=[("qkv", scr)])

        w.barrier()
        if PHASES < 2:
            return

        def attn_qtile(hp, g, tl, qs):
            nkt = g + 1
            nch = (nkt + 3) // 4
            for p in range(4):
                qb = (qs * 4 + p) % 4
                w.op("sync", lambda e, hp=hp, p=p, tl=tl, qb=qb: e.dma_start(
                    out=SB["qt"][:, qb * 128:(qb + 1) * 128],
                    in_=dram["q_scr"][hp * 512 + p * 128:
                                      hp * 512 + (p + 1) * 128,
                                      tl * 128:(tl + 1) * 128]),
                    sem="ld", n=16,
                    reads=[("qkv", "q_scr")],
                    writes=[("qt", qb)])
                for c in range(nch):
                    klo = c * 512
                    for ab in (0, 1):
                        bk = (c % 2) * 2 + ab
                        bank = ps[bk]
                        lhs = SB["qt"][ab * 64:ab * 64 + 64,
                                       qb * 128:(qb + 1) * 128]
                        rhs = SB["kT"][ab * 64:ab * 64 + 64,
                                       p * S + klo:p * S + klo + 512]
                        w.op("tensor", lambda e, bank=bank, lhs=lhs, rhs=rhs:
                             e.matmul(bank[:, 0:512], lhs, rhs,
                                      start=True, stop=True),
                             sem="pe",
                             reads=[("qt", qb), "kT"],
                             writes=[("ps", bk)])
                    for ab in (0, 1):
                        bk = (c % 2) * 2 + ab
                        bank = ps[bk]
                        dst = SB["srow"][:, ab * 4608 + klo:
                                         ab * 4608 + klo + 512]
                        if c == nch - 1:
                            cms = SB["c_cm4"][:, (g % 4) * 512:
                                              (g % 4) * 512 + 512]
                            w.op("vector", lambda e, dst=dst, bank=bank,
                                 cms=cms: e.tensor_add(
                                     dst, bank[:, 0:512], cms),
                                 sem="dve", reads=[("ps", bk), "c_cm4"],
                                 writes=[("srow", ab, c)])
                        else:
                            w.op("vector", lambda e, dst=dst, bank=bank:
                                 e.tensor_copy(dst, bank[:, 0:512]),
                                 sem="dve", reads=[("ps", bk)],
                                 writes=[("srow", ab, c)])
                for ab in (0, 1):
                    w.op("vector", lambda e, ab=ab, nch=nch: e.reduce_max(
                        SB["mx"][:, ab:ab + 1],
                        SB["srow"][:, ab * 4608:ab * 4608 + nch * 512],
                        axis=AX.X),
                        sem="dve",
                        reads=[("srow", ab, c) for c in range(nch)],
                        writes=[("mx", ab)])
                    w.op("vector", lambda e, ab=ab: e.tensor_scalar_mul(
                        SB["negm"][:, ab:ab + 1], SB["mx"][:, ab:ab + 1],
                        -1.0), sem="dve", reads=[("mx", ab)],
                        writes=[("negm", ab)])
                for ab in (0, 1):
                    w.op("scalar", lambda e, ab=ab, nch=nch: e.activation(
                        SB["prow"][:, ab * 4608:ab * 4608 + nch * 512],
                        SB["srow"][:, ab * 4608:ab * 4608 + nch * 512],
                        AF.Exp, bias=SB["negm"][:, ab:ab + 1],
                        accum_out=SB["lcol"][:, ab:ab + 1]),
                        sem="act",
                        reads=([("srow", ab, c) for c in range(nch)]
                               + [("negm", ab)]),
                        writes=[("prow", ab), ("lcol", ab)])

                def _recip(e):
                    with nc.allow_low_precision(
                            reason="bf16 softmax normalizer, within tol"):
                        return e.reciprocal(SB["rcol"][:, 0:2],
                                            SB["lcol"][:, 0:2])
                w.op("vector", _recip, sem="dve",
                     reads=[("lcol", 0), ("lcol", 1)], writes=["rcol"])
                w.op("tensor", lambda e: e.transpose(
                    ps[4][:].bitcast(BF16)[0:2, 0:128], SB["rcol"][:, 0:2],
                    SB["c_idt"][:]), sem="pe",
                    reads=["rcol", "c_idt"], writes=[("ps", 4)])
                w.op("vector", lambda e: e.tensor_copy(
                    SB["rrow"][0:2, 0:128],
                    ps[4][:].bitcast(BF16)[0:2, 0:128]),
                    sem="dve", reads=[("ps", 4)], writes=["rrow"])
                for ab in (0, 1):
                    w.op("tensor", lambda e, ab=ab: e.matmul(
                        ps[4][ab * 64:ab * 64 + 64, 128:256],
                        SB["c_e2"][:, ab * 64:ab * 64 + 64],
                        SB["rrow"][0:2, 0:128], start=True, stop=True),
                        sem="pe", reads=["rrow", "c_e2"],
                        writes=[("ps", 4)])
                w.op("scalar", lambda e: e.activation(
                    SB["rbc"][:, 0:128], ps[4][:, 128:256], AF.Copy),
                    sem="act", reads=[("ps", 4)],
                    writes=["rbc"])
                for kt in range(nkt):
                    sl = kt % 2
                    for ab in (0, 1):
                        w.op("tensor", lambda e, kt=kt, ab=ab, sl=sl:
                             e.transpose(
                                 ps[6 + sl][:].bitcast(BF16)[
                                     :, ab * 256:ab * 256 + 128],
                                 SB["prow"][:, ab * 4608 + kt * 128:
                                            ab * 4608 + (kt + 1) * 128],
                                 SB["c_idt"][:]),
                             sem="pe",
                             reads=[("prow", ab), "c_idt"],
                             writes=[("ps", 6 + sl)])
                    w.op("vector", lambda e, sl=sl: e.tensor_copy(
                        SB["ptsb"][:, sl * 512:sl * 512 + 512],
                        ps[6 + sl][:].bitcast(BF16)[:, 0:512]),
                        sem="dve",
                        reads=[("ps", 6 + sl)],
                        writes=[("ptsb", sl)])
                    for ab in (0, 1):
                        w.op("tensor", lambda e, kt=kt, ab=ab, sl=sl, hp=hp,
                             p=p, nkt=nkt: e.matmul(
                                 ps[5][ab * 64:ab * 64 + 64, 0:128],
                                 SB["vT"][:, kt * 512 + (2 * p + ab) * 64:
                                          kt * 512 + (2 * p + ab) * 64 + 64],
                                 SB["ptsb"][:, sl * 512 + ab * 256:
                                            sl * 512 + ab * 256 + 128],
                                 start=(kt == 0), stop=(kt == nkt - 1),
                                 tile_position=(0, ab * 64)),
                             sem="pe" if (kt == nkt - 1 and ab == 1) else None,
                             reads=[("ptsb", sl), "vT"],
                             writes=([("ps", 5)]
                                     if ((kt == 0 and ab == 0)
                                         or (kt == nkt - 1 and ab == 1))
                                     else ()))
                w.op("vector", lambda e, p=p, qs=qs: e.scalar_tensor_tensor(
                    SB["astg"][:, (qs % 2) * 512 + p * 128:
                               (qs % 2) * 512 + (p + 1) * 128],
                    ps[5][:, 0:128], 0.0, SB["rbc"][:, 0:128],
                    ALU.add, ALU.mult),
                    sem="dve", reads=[("ps", 5), "rbc"],
                    writes=[("astg", qs % 2, p)])
            w.op("sync", lambda e, hp=hp, tl=tl, qs=qs: e.dma_start(
                out=dram["attn_scr"][hp * 512:(hp + 1) * 512,
                                     tl * 128:(tl + 1) * 128]
                    .rearrange("(b p) f -> p b f", p=128),
                in_=SB["astg"][:, (qs % 2) * 512:(qs % 2 + 1) * 512]
                    .rearrange("p (b f) -> p b f", b=4)),
                sem="ld", n=16,
                reads=[("astg", qs % 2, p) for p in range(4)],
                writes=[("att_out",)])

        for hp in (0, 1):
            for p in range(4):
                w.op("sync", lambda e, hp=hp, p=p: e.dma_start(
                    out=SB["kT"][:, p * S:(p + 1) * S],
                    in_=dram["kt_scr"][hp * 512 + p * 128:
                                       hp * 512 + (p + 1) * 128, :]),
                    sem="ld", n=16,
                    reads=[("qkv", "kt_scr")],
                    writes=["kT"])
            for kt in range(32):
                w.op("sync", lambda e, hp=hp, kt=kt: e.dma_start_transpose(
                    SB["vT"][:, kt * 512:(kt + 1) * 512],
                    dram["vt_scr"][hp * 512:(hp + 1) * 512,
                                   kt * 128:(kt + 1) * 128]),
                    sem="ld", n=16,
                    reads=[("qkv", "vt_scr")],
                    writes=["vT"])
            for j in range(8):
                snapst = w.snap()
                w.if_hg0_begin()
                ga, gb_ = QPAIRS[0][j]
                attn_qtile(hp, ga, j, 2 * j)
                attn_qtile(hp, gb_, 8 + j, 2 * j + 1)
                cA = dict(w.cnt)
                rA = _copy.deepcopy(w.res)
                wmA = dict(w.wm)
                w.restore(snapst)
                w.else_begin()
                ga, gb_ = QPAIRS[1][j]
                attn_qtile(hp, ga, j, 2 * j)
                attn_qtile(hp, gb_, 8 + j, 2 * j + 1)
                w.endif()
                w.merge_arm(cA, rA, wmA)

        w.barrier()
        if PHASES < 3:
            return

        for tt in range(16):
            for d in range(8):
                w.op("tensor", lambda e, tt=tt, d=d: e.matmul(
                    ps[7][:, 0:4],
                    SB["xnT_own"][:, d * TOK + tt * 128:
                                  d * TOK + (tt + 1) * 128],
                    SB["c_wrt"][:, d * 4:(d + 1) * 4],
                    start=(d == 0), stop=(d == 7)),
                    sem="pe" if d == 7 else None,
                    reads=([("xnT_own", d, tt // 4), "c_wrt"]
                           if d == 0 else ()),
                    writes=[("ps", 7)] if d in (0, 7) else ())
            w.op("vector", lambda e: e.tensor_copy(
                SB["f_lg"][:, 0:4], ps[7][:, 0:4]),
                sem="dve", reads=[("ps", 7)], writes=[("f_lg", 0)])
            w.op("vector", lambda e: e.reduce_max(
                SB["f_lg"][:, 8:9], SB["f_lg"][:, 0:4], axis=AX.X),
                sem="dve", reads=[("f_lg", 0)], writes=[("f_lg", 1)])
            w.op("vector", lambda e: e.tensor_scalar(
                SB["f_msk"][:], SB["f_lg"][:, 0:4], SB["f_lg"][:, 8:9],
                None, ALU.is_ge), sem="dve",
                reads=[("f_lg", 0), ("f_lg", 1)], writes=["f_msk"])
            for ex in range(4):
                w.op("tensor", lambda e, ex=ex: e.transpose(
                    ps[7][:].bitcast(BF16)[0:1, 512 + ex * 128:
                                           512 + (ex + 1) * 128],
                    SB["f_msk"][:, ex:ex + 1], SB["c_idt"][:]),
                    sem="pe", reads=["f_msk", "c_idt"],
                    writes=[("ps", 7)])
            for ex in range(4):
                w.op("vector", lambda e, ex=ex, tt=tt: e.tensor_copy(
                    SB[f"f_mr{ex}"][0:1, tt * 128:(tt + 1) * 128],
                    ps[7][:].bitcast(BF16)[0:1, 512 + ex * 128:
                                           512 + (ex + 1) * 128]),
                    sem="dve", reads=[("ps", 7)],
                    writes=[("f_mr", ex, tt)])
        for ex in range(4):
            w.op("vector", lambda e, ex=ex: e.tensor_scalar(
                SB[f"f_ar{ex}"][0:1, :], SB[f"f_mr{ex}"][0:1, :],
                -1.0, 1.0e4, ALU.add, ALU.mult),
                sem="dve", reads=[("f_mr", ex, tt) for tt in range(16)],
                writes=[("f_ar", ex)])

        for icb in range(32):
            for ex in range(4):
                buf = (icb * 4 + ex) % 3
                w.op("gpsimd", lambda e, ex=ex, icb=icb, buf=buf: e.dma_start(
                    out=SB["f_w1"][:, buf * 1024:(buf + 1) * 1024],
                    in_=dram["w1r"][ex, icb]), sem="gp", n=16,
                    writes=[("f_w1", buf)])
                for tc in range(4):
                    bk = (icb * 16 + ex * 4 + tc) % 4
                    bank = ps[bk]
                    w.op("tensor", lambda e, ex=ex, tc=tc, bank=bank: e.matmul(
                        bank[:, 0:512], SB["c_ones"][0:1, :],
                        SB[f"f_ar{ex}"][0:1, tc * 512:(tc + 1) * 512],
                        start=True, stop=False),
                        reads=[("f_ar", ex), "c_ones", ("f_w1", buf)],
                        writes=[("ps", bk)])
                    for d in range(8):
                        w.op("tensor", lambda e, d=d, tc=tc, buf=buf,
                             bank=bank: e.matmul(
                                 bank[:, 0:512],
                                 SB["f_w1"][:, buf * 1024 + d * 128:
                                            buf * 1024 + (d + 1) * 128],
                                 SB["xnT_own"][:, d * TOK + tc * 512:
                                               d * TOK + (tc + 1) * 512],
                                 start=False, stop=(d == 7)),
                             sem="pe" if d == 7 else None,
                             reads=[("xnT_own", d, tc)] if d == 0 else (),
                             writes=[("ps", bk)] if d == 7 else ())
                    w.op("scalar", lambda e, ex=ex, tc=tc, icb=icb, bank=bank:
                         e.activation(
                             SB["f_h"][:, (ex * 4 + tc) * 512:
                                       (ex * 4 + tc + 1) * 512],
                             bank[:, 0:512], AF.Gelu,
                             bias=SB["c_b1f"][:, ex * 32 + icb:
                                              ex * 32 + icb + 1]),
                         sem="act", reads=[("ps", bk), "c_b1f"],
                         writes=[("f_h", ex, tc)])
            for tc in range(4):
                if debug:
                    for ex in range(4):
                        w.op("gpsimd", lambda e, icb=icb, tc=tc, ex=ex:
                             e.dma_start(
                                 out=dram[f"int_e{ex}"][icb, tc],
                                 in_=SB["f_h"][:, (ex * 4 + tc) * 512:
                                               (ex * 4 + tc + 1) * 512]),
                             sem="gp", n=16, reads=[("f_h", ex, tc)])
                for ex in (1, 2, 3):
                    w.op("vector", lambda e, tc=tc, ex=ex: e.tensor_add(
                        SB["f_h"][:, tc * 512:(tc + 1) * 512],
                        SB["f_h"][:, tc * 512:(tc + 1) * 512],
                        SB["f_h"][:, (ex * 4 + tc) * 512:
                                  (ex * 4 + tc + 1) * 512]),
                        sem="dve",
                        reads=[("f_h", 0, tc), ("f_h", ex, tc)],
                        writes=[("f_h", 0, tc)])
                w.op("gpsimd", lambda e, icb=icb, tc=tc: e.dma_start(
                    out=dram["inter_scr"][icb, tc],
                    in_=SB["f_h"][:, tc * 512:(tc + 1) * 512]),
                    sem="gp", n=16, reads=[("f_h", 0, tc)],
                    writes=[("int_out", icb, tc)])

        w.barrier()
        if PHASES < 4:
            return

        for hh in (0, 1):
            for icb in range(32):
                w.op("gpsimd", lambda e, icb=icb, hh=hh: e.dma_start(
                    out=SB["w_wo"][:, icb * 512:(icb + 1) * 512],
                    in_=dram["wo"][icb * 128:(icb + 1) * 128,
                                   hh * 512:(hh + 1) * 512]),
                    sem="gp", n=16, writes=[("w_wo", icb)])
            for tc in range(4):
                for icb in range(32):
                    w.op("sync", lambda e, icb=icb, tc=tc: e.dma_start(
                        out=SB["w_int"][:, icb * 512:(icb + 1) * 512],
                        in_=dram["inter_scr"][icb, tc]),
                        sem="ld", n=16,
                        reads=[("int_out", icb, tc)],
                        writes=[("w_int", icb)])
                for ttl in range(4):
                    tt = tc * 4 + ttl
                    bank = ps[4 + tt % 2]
                    xslot = tt % 4
                    w.op("sync", lambda e, tt=tt, hh=hh, xslot=xslot:
                         e.dma_start(
                             out=SB["w_xn"][:, xslot * 512:
                                            (xslot + 1) * 512],
                             in_=dram["xn_own_scr"][tt * 128:(tt + 1) * 128,
                                                    hh * 512:(hh + 1) * 512]),
                         sem="ld", n=16,
                         reads=[("scr", "xn_own_scr", tt)],
                         writes=[("w_xn", xslot)])
                    w.op("tensor", lambda e, hh=hh, bank=bank: e.matmul(
                        bank[:, 0:512], SB["c_ones"][0:1, :],
                        SB["c_bo"][0:1, hh * 512:(hh + 1) * 512],
                        start=True, stop=False),
                        reads=(["c_ones", "c_bo"]
                               + [("w_int", icb) for icb in range(32)]
                               + [("w_wo", icb) for icb in range(32)]),
                        writes=[("ps", 4 + tt % 2)])
                    for icb in range(32):
                        w.op("tensor", lambda e, icb=icb, ttl=ttl, bank=bank:
                             e.matmul(
                                 bank[:, 0:512],
                                 SB["w_int"][:, icb * 512 + ttl * 128:
                                             icb * 512 + ttl * 128 + 128],
                                 SB["w_wo"][:, icb * 512:(icb + 1) * 512],
                                 start=False, stop=(icb == 31)),
                             sem="pe" if icb == 31 else None,
                             writes=([("ps", 4 + tt % 2)]
                                     if icb == 31 else ()))
                    w.op("vector", lambda e, bank=bank, xslot=xslot:
                         e.tensor_add(
                             SB["w_yh"][:, xslot * 512:(xslot + 1) * 512],
                             bank[:, 0:512],
                             SB["w_xn"][:, xslot * 512:(xslot + 1) * 512]),
                         sem="dve",
                         reads=[("ps", 4 + tt % 2), ("w_xn", xslot)],
                         writes=[("w_yh", xslot)])
                    w.op("sync", lambda e, tt=tt, hh=hh, xslot=xslot:
                         e.dma_start(
                             out=dram["y_scr"][tt * 128:(tt + 1) * 128,
                                               hh * 512:(hh + 1) * 512],
                             in_=SB["w_yh"][:, xslot * 512:
                                            (xslot + 1) * 512]),
                         sem="ld", n=16, reads=[("w_yh", xslot)],
                         writes=[("y_out", tt, hh)])

        for tt in range(16):
            b = tt % 2
            yb = SB["w_yb"][:, b * 1024:(b + 1) * 1024]
            st = SB["w_st"][:, b * 16:(b + 1) * 16]
            wt = SB["w_t"][:, b * 1024:(b + 1) * 1024]
            ty, twt = ("w_yb", b), ("w_t", b)
            tst = lambda k: ("w_st", b, k)
            w.op("sync", lambda e, tt=tt, yb=yb: e.dma_start(
                out=yb, in_=dram["y_scr"][tt * 128:(tt + 1) * 128, :]),
                sem="ld", n=16,
                reads=[("y_out", tt, 0), ("y_out", tt, 1)], writes=[ty])
            w.op("vector", lambda e, yb=yb, st=st: e.reduce_sum(
                st[:, 0:1], yb, axis=AX.X), sem="dve",
                reads=[ty], writes=[tst(0)])
            w.op("scalar", lambda e, yb=yb, st=st: e.activation(
                SB["w_dump"][:], yb, AF.Square, accum_out=st[:, 1:2]),
                sem="act", reads=[ty], writes=["w_dump", tst(1)])
            w.op("vector", lambda e, st=st: e.tensor_scalar_mul(
                st[:, 2:3], st[:, 0:1], -1.0 / E), sem="dve",
                reads=[tst(0)], writes=[tst(2)])
            w.op("vector", lambda e, st=st: e.tensor_mul(
                st[:, 3:4], st[:, 2:3], st[:, 2:3]), sem="dve",
                reads=[tst(2)], writes=[tst(3)])
            w.op("vector", lambda e, st=st: e.tensor_scalar(
                st[:, 4:5], st[:, 1:2], 1.0 / E, None, ALU.mult),
                sem="dve", reads=[tst(1)], writes=[tst(4)])
            w.op("vector", lambda e, st=st: e.tensor_sub(
                st[:, 5:6], st[:, 4:5], st[:, 3:4]), sem="dve",
                reads=[tst(3), tst(4)], writes=[tst(5)])
            w.op("scalar", lambda e, st=st: e.activation(
                st[:, 6:7], st[:, 5:6], AF.Ln, bias=SB["c_eps"][:, 1:2]),
                sem="act", reads=[tst(5), "c_eps"], writes=[tst(6)])
            w.op("scalar", lambda e, st=st: e.activation(
                st[:, 7:8], st[:, 6:7], AF.Exp, scale=-0.5),
                sem="act", reads=[tst(6)], writes=[tst(7)])
            w.op("vector", lambda e, yb=yb, st=st, wt=wt: e.tensor_scalar(
                wt, yb, st[:, 2:3], st[:, 7:8], ALU.add, ALU.mult),
                sem="dve", reads=[ty, tst(2), tst(7)], writes=[twt])
            w.op("vector", lambda e, wt=wt: e.tensor_mul(
                wt, wt, SB["c_gb"][:]), sem="dve",
                reads=[twt, "c_gb"], writes=[twt])
            w.op("vector", lambda e, tt=tt, wt=wt: e.tensor_add(
                SB["f_out"][:, tt * 1024:(tt + 1) * 1024],
                wt, SB["c_bb"][:]), sem="dve",
                reads=[twt, "c_bb"], writes=[("f_out", tt)])

        w.barrier()
        if PHASES < 5:
            return

        for ac in range(8):
            w.op("sync", lambda e, ac=ac: e.dma_start(
                out=SB["c_ac"][:, ac * 2048:(ac + 1) * 2048],
                in_=dram["attn_scr"][ac * 128:(ac + 1) * 128, :]),
                sem="ld", n=16, writes=[("c_ac", ac)])
        w.op("gpsimd", lambda e: e.dma_start(
            out=SB["c_wp"][:].rearrange("p (c n) -> p c n", c=8),
            in_=dram["wprojT"].rearrange("(c p) n -> p c n", p=128)),
            sem="gp", n=16, writes=["c_wp"])
        for tt in range(16):
            b = tt % 2
            xo = SB["c_x"][:, b * E:(b + 1) * E]
            oo = SB["c_o"][:, b * E:(b + 1) * E]
            tmp = SB["c_tmp"][:, b * E:(b + 1) * E]
            w.op("sync", lambda e, tt=tt, xo=xo: e.dma_start(
                out=xo, in_=dram["x_own"][tt * 128:(tt + 1) * 128, :]),
                sem="ld", n=16, writes=[("c_x", b)])
            for hh in (0, 1):
                bk = b * 2 + hh
                bank = ps[bk]
                w.op("tensor", lambda e, hh=hh, bank=bank: e.matmul(
                    bank[:, 0:512], SB["c_ones"][0:1, :],
                    SB["c_bproj"][0:1, hh * 512:(hh + 1) * 512],
                    start=True, stop=False),
                    reads=(["c_ones", "c_bproj", "c_wp"]
                           + [("c_ac", ac) for ac in range(8)]),
                    writes=[("ps", bk)])
                for ac in range(8):
                    w.op("tensor", lambda e, ac=ac, hh=hh, tt=tt, bank=bank:
                         e.matmul(
                             bank[:, 0:512],
                             SB["c_ac"][:, ac * 2048 + tt * 128:
                                        ac * 2048 + (tt + 1) * 128],
                             SB["c_wp"][:, ac * 1024 + hh * 512:
                                        ac * 1024 + (hh + 1) * 512],
                             start=False, stop=(ac == 7)),
                         sem="pe" if ac == 7 else None,
                         writes=[("ps", bk)] if ac == 7 else ())
                w.op("vector", lambda e, hh=hh, xo=xo, tmp=tmp, bank=bank:
                     e.tensor_add(tmp[:, hh * 512:(hh + 1) * 512],
                                  bank[:, 0:512],
                                  xo[:, hh * 512:(hh + 1) * 512]),
                     sem="dve", reads=[("ps", bk), ("c_x", b)],
                     writes=[("c_tmp", b, hh)])
                w.op("vector", lambda e, hh=hh, tt=tt, oo=oo, tmp=tmp:
                     e.tensor_add(
                         oo[:, hh * 512:(hh + 1) * 512],
                         tmp[:, hh * 512:(hh + 1) * 512],
                         SB["f_out"][:, tt * 1024 + hh * 512:
                                     tt * 1024 + (hh + 1) * 512]),
                     sem="dve", reads=[("c_tmp", b, hh), ("f_out", tt)],
                     writes=[("c_o", b, hh)])
            w.op("scalar", lambda e, tt=tt, oo=oo: e.dma_start(
                out=dram["out"][tt * 128:(tt + 1) * 128, :], in_=oo),
                sem="sc", n=16, reads=[("c_o", b, 0), ("c_o", b, 1)],
                writes=[("out", tt)])

        w.barrier()

    with stack:
        @block.sync
        def _(eng):
            walk(W(nc, "sync", eng, sems))

        @block.tensor
        def _(eng):
            walk(W(nc, "tensor", eng, sems))

        @block.vector
        def _(eng):
            walk(W(nc, "vector", eng, sems))

        @block.scalar
        def _(eng):
            walk(W(nc, "scalar", eng, sems))

        @block.gpsimd
        def _(eng):
            walk(W(nc, "gpsimd", eng, sems))

    nc.finalize()
    return nc


_NC_CACHE = {}


def _get_nc(debug=False):
    if debug not in _NC_CACHE:
        _NC_CACHE[debug] = build_program(debug)
    return _NC_CACHE[debug]


def _cm4():
    f32 = np.float32
    cm = np.zeros((128, 4 * 512), dtype=f32)
    p = np.arange(128).reshape(128, 1)
    f = np.arange(512).reshape(1, 512)
    for j in range(4):
        cm[:, j * 512:(j + 1) * 512] = np.where(f <= p + 128 * j, 0.0, -1e9)
    return cm


def host_inputs(inputs, debug=False):
    x = np.asarray(inputs["L0_attn_hidden_states"])
    w_attn = np.asarray(inputs["L0_attn_w_attn"])
    b_attn = np.asarray(inputs["L0_attn_b_attn"])
    w_proj = np.asarray(inputs["L0_attn_w_proj"])
    b_proj = np.asarray(inputs["L0_attn_b_proj"])
    W_router = np.asarray(inputs["L0_ffn_W_router"])
    W1 = np.asarray(inputs["L0_ffn_W1"])
    b1 = np.asarray(inputs["L0_ffn_b1"])
    Wo = np.asarray(inputs["L0_ffn_Wo"])
    bo = np.asarray(inputs["L0_ffn_bo"])
    gam = np.asarray(inputs["L0_ffn_ln_gamma"])
    bet = np.asarray(inputs["L0_ffn_ln_beta"])

    f32 = np.float32
    com = {
        "wq": np.ascontiguousarray(w_attn[0:E].T, dtype=f32),
        "wk": np.ascontiguousarray(w_attn[E:2 * E].T, dtype=f32),
        "wv": np.ascontiguousarray(w_attn[2 * E:3 * E].T, dtype=f32),
        "bq": np.ascontiguousarray(b_attn[0:E].reshape(8, 128).T, dtype=f32),
        "bk": np.ascontiguousarray(b_attn[E:2 * E].reshape(8, 128).T,
                                   dtype=f32),
        "bv": np.ascontiguousarray(b_attn[2 * E:3 * E].reshape(8, 128).T,
                                   dtype=f32),
        "wprojT": np.ascontiguousarray(w_proj.T, dtype=f32),
        "bproj_row": np.ascontiguousarray(b_proj.reshape(1, E), dtype=f32),
        "w_router": np.ascontiguousarray(W_router, dtype=f32),
        "w1r": np.ascontiguousarray(
            W1.reshape(NE, 8, 128, 32, 128).transpose(0, 3, 2, 1, 4)
              .reshape(NE, 32, 128, 1024), dtype=f32),
        "b1f": np.ascontiguousarray(
            b1.reshape(NE, 32, 128).transpose(2, 0, 1).reshape(128, 128),
            dtype=f32),
        "wo": np.ascontiguousarray(Wo, dtype=f32),
        "bo_row": np.ascontiguousarray(bo.reshape(1, E), dtype=f32),
        "gb": np.ascontiguousarray(np.tile(gam.reshape(1, E), (128, 1)),
                                   dtype=f32),
        "bb": np.ascontiguousarray(np.tile(bet.reshape(1, E), (128, 1)),
                                   dtype=f32),
        "idt": np.eye(128, dtype=f32),
        "ones_row": np.ones((1, 128), dtype=f32),
        "e2": np.concatenate(
            [np.tile(np.array([[1.0], [0.0]], f32), (1, 64)),
             np.tile(np.array([[0.0], [1.0]], f32), (1, 64))],
            axis=1),
        "cm4": _cm4(),
        "epsln": np.tile(np.array([[LN1_EPS, LN2_EPS]], f32), (128, 1)),
    }
    maps = []
    for c in range(8):
        b_, hg = c // 2, c % 2
        rows = _qrows(hg)
        m = dict(com)
        m["x_full"] = np.ascontiguousarray(x[b_], dtype=f32)
        m["x_own"] = np.ascontiguousarray(x[b_][rows], dtype=f32)
        maps.append(m)
    return maps


def kernel(**inputs):
    nc = _get_nc(False)
    in_maps = host_inputs(inputs)
    res = run_bass_kernel_spmd(nc, in_maps, core_ids=list(range(8)))
    out = np.empty((B, S, E), dtype=np.float32)
    for c in range(8):
        b_, hg = c // 2, c % 2
        out[b_][_qrows(hg)] = res.results[c]["out"]
    return out
